# version 01
# baseline (speedup 1.0000x reference)
import numpy as np
import ml_dtypes
from contextlib import ExitStack

import concourse.bass as bass
import concourse.mybir as mybir
from concourse.bass_utils import run_bass_kernel_spmd

F32 = mybir.dt.float32
BF16 = mybir.dt.bfloat16
I32 = mybir.dt.int32
AF = mybir.ActivationFunctionType
ALU = mybir.AluOpType
AX = mybir.AxisListType
BFNP = ml_dtypes.bfloat16

D_MODEL = 1024
HID = 4096
SEQ = 2048
NB = 16
NCORES = 8
BPC = NB // NCORES
RT = BPC * SEQ
NTCH = RT // 512
H = 512
CC = 256
NCH = SEQ // CC
EPS = 1e-5
INV2PI = float(1.0/(2*np.pi))
MAGIC = 12582912.0
TWO_PI = float(2*np.pi)

ENGINES = ("pe", "act", "dve", "pool", "sp")
_SEC = {"pe": "tensor", "act": "scalar", "dve": "vector", "pool": "gpsimd", "sp": "sync"}


class Op:
    __slots__ = ("idx", "eng", "fn", "deps", "inc_amt", "cum", "name", "semkey")

    def __init__(self, idx, eng, fn, deps, inc_amt, name):
        self.idx = idx; self.eng = eng; self.fn = fn
        self.deps = deps; self.inc_amt = inc_amt
        self.cum = None; self.name = name


class Sched:
    def __init__(self, nc):
        self.nc = nc
        self.ops = []
        self.per_eng = {e: [] for e in ENGINES}
        self.last_writer = {}
        self.readers = {}
        self.needed = set()

    def add(self, eng, fn, reads=(), writes=(), deps=(), dma=False, name=""):
        dep_set = set(deps)
        for b in reads:
            if b in self.last_writer:
                dep_set.add(self.last_writer[b])
        for b in writes:
            if b in self.last_writer:
                dep_set.add(self.last_writer[b])
            dep_set.update(self.readers.get(b, ()))
        dep_set.discard(len(self.ops))
        op = Op(len(self.ops), eng, fn, sorted(dep_set), 16 if dma else 1, name)
        self.ops.append(op)
        self.per_eng[eng].append(op)
        for b in writes:
            self.last_writer[b] = op.idx
            self.readers[b] = set()
        for b in reads:
            self.readers.setdefault(b, set()).add(op.idx)
        return op.idx

    def emit(self, es, block):
        NDSEM = 12
        for op in self.ops:
            if op.inc_amt == 16:
                self.needed.add(op.idx)
            for d in op.deps:
                dep = self.ops[d]
                if dep.eng != op.eng or dep.inc_amt == 16 or dep.eng in ("dve", "act", "pool"):
                    self.needed.add(d)
        dma_ord = {e: 0 for e in ENGINES}
        cum = {}
        prev_slot_op = {}
        for op in self.ops:
            if op.inc_amt == 16:
                slot = dma_ord[op.eng] % NDSEM
                dma_ord[op.eng] += 1
                key = ("D", op.eng, slot)
                op.semkey = key
                cum[key] = cum.get(key, 0) + 16
                op.cum = cum[key]
                pk = (op.eng, slot)
                if pk in prev_slot_op:
                    op.deps = sorted(set(op.deps) | {prev_slot_op[pk]})
                prev_slot_op[pk] = op.idx
            elif op.idx in self.needed:
                key = ("C", op.eng)
                op.semkey = key
                cum[key] = cum.get(key, 0) + 1
                op.cum = cum[key]
            else:
                op.semkey = None
        sems = {}
        for key in cum:
            name = "_".join(str(x) for x in key)
            sems[key] = es.enter_context(self.nc.semaphore(f"S{name}"))
        for eng in ENGINES:
            ops = self.per_eng[eng]
            if not ops:
                continue
            sec_dec = getattr(block, _SEC[eng])

            def emit_engine(sec, ops=ops, eng=eng):
                hwm = {}
                for op in ops:
                    for d in op.deps:
                        dep = self.ops[d]
                        if dep.semkey is None:
                            assert dep.eng == eng, f"dep {dep.name} of {op.name} lacks sem"
                            continue
                        if dep.eng == eng and dep.inc_amt != 16 and eng == "pe":
                            continue
                        if hwm.get(dep.semkey, 0) < dep.cum:
                            sec.wait_ge(sems[dep.semkey], dep.cum)
                            hwm[dep.semkey] = dep.cum
                    last = op.fn(sec)
                    if op.semkey is not None and (op.idx in self.needed):
                        assert last is not None, f"op {op.name} returned no instruction"
                        last.then_inc(sems[op.semkey], op.inc_amt)
            sec_dec(emit_engine)


_CONST_CACHE = {}

def host_constants():
    if _CONST_CACHE:
        return _CONST_CACHE
    d = np.arange(D_MODEL)
    ang = 2*np.pi*np.outer(d, d)/D_MODEL
    _CONST_CACHE["cmat"] = np.cos(ang).astype(BFNP)
    _CONST_CACHE["smat"] = np.sin(ang).astype(BFNP)
    F2 = 2*CC
    m = np.arange(CC); f = np.arange(CC+1)
    a2 = 2*np.pi*np.outer(m, f)/F2
    fwdc = np.zeros((CC, 384), np.float32); fwds = np.zeros((CC, 384), np.float32)
    fwdc[:, :CC+1] = np.cos(a2)
    fwds[:, :CC+1] = -np.sin(a2)
    _CONST_CACHE["fwdc"] = fwdc.astype(BFNP)
    _CONST_CACHE["fwds"] = fwds.astype(BFNP)
    cf = np.where((f == 0) | (f == CC), 1.0, 2.0)
    ai = 2*np.pi*np.outer(f, np.arange(CC))/F2
    invc = np.zeros((384, CC), np.float32); invs = np.zeros((384, CC), np.float32)
    invc[:CC+1, :] = cf[:, None]*np.cos(ai)/F2
    invs[:CC+1, :] = -cf[:, None]*np.sin(ai)/F2
    _CONST_CACHE["invc"] = invc.astype(BFNP)
    _CONST_CACHE["invs"] = invs.astype(BFNP)
    _CONST_CACHE["identb"] = np.eye(128, dtype=BFNP)
    _CONST_CACHE["identf"] = np.eye(128, dtype=np.float32)
    _CONST_CACHE["jrow"] = np.tile(np.arange(257, dtype=np.float32), (128, 1))
    return _CONST_CACHE


def build_program(debug=()):
    nc = bass.Bass("TRN2", target_bir_lowering=False)
    dram = {}
    def din(name, shape, dt=F32):
        dram[name] = nc.dram_tensor(name, shape, dt, kind="ExternalInput").ap()
    din("x0", [D_MODEL, RT])
    din("wxg", [D_MODEL, HID])
    din("wdin", [D_MODEL, H])
    din("wgate", [H, HID])
    din("wout", [HID, D_MODEL])
    din("wfr", [D_MODEL, D_MODEL]); din("wfi", [D_MODEL, D_MODEL])
    din("crt", [512, 512]); din("cit", [512, 512])
    din("lr", [128, 4]); din("li", [128, 4])
    din("dvec", [1, 512])
    din("n1w", [128, 8]); din("n1b", [128, 8])
    din("n2w", [1, 512]); din("n2b", [1, 512])
    din("fbias", [1, D_MODEL]); din("bout", [1, D_MODEL])
    din("cmat", [D_MODEL, D_MODEL], BF16); din("smat", [D_MODEL, D_MODEL], BF16)
    din("fwdc", [CC, 384], BF16); din("fwds", [CC, 384], BF16)
    din("invc", [384, CC], BF16); din("invs", [384, CC], BF16)
    din("identf", [128, 128])
    din("jrow", [128, 257])
    yout = nc.dram_tensor("y", [D_MODEL, RT], F32, kind="ExternalOutput").ap()
    scr = {}
    scr["u1"] = nc.dram_tensor("scr_u1", [D_MODEL, RT], BF16, kind="Internal").ap()
    scr["x1v"] = nc.dram_tensor("scr_x1v", [D_MODEL, RT], F32, kind="Internal").ap()
    scr["xln"] = nc.dram_tensor("scr_xln", [D_MODEL, RT], BF16, kind="Internal").ap()
    scr["x1raw"] = nc.dram_tensor("scr_x1raw", [D_MODEL, RT], F32, kind="Internal").ap()
    for nm in ("pt_hi", "pt_lo", "qt_hi", "qt_lo", "m_hi", "m_lo",
               "wrhi", "wrlo", "wihi", "wilo"):
        scr[nm] = nc.dram_tensor("scr_" + nm, [D_MODEL, D_MODEL], BF16, kind="Internal").ap()

    dbg = {}
    def dout(name, shape, dt):
        dbg[name] = nc.dram_tensor("dbg_" + name, shape, dt, kind="ExternalOutput").ap()
    shapes = {
        "mmat": ([D_MODEL, D_MODEL], BF16),
        "pthi": ([D_MODEL, D_MODEL], BF16),
        "ptlo": ([D_MODEL, D_MODEL], BF16),
        "qthi": ([D_MODEL, D_MODEL], BF16),
        "wrhi": ([D_MODEL, D_MODEL], BF16),
        "wrlo": ([D_MODEL, D_MODEL], BF16),
        "mlo": ([D_MODEL, D_MODEL], BF16),
        "x1v": ([D_MODEL, RT], F32),
        "x1raw": ([D_MODEL, RT], F32),
        "ffnps": ([128, 512], F32),
        "pqltap": ([128, 8*128], BF16),
        "bfbtap": ([128, 8*512], BF16),
        "u1": ([D_MODEL, RT], BF16),
        "xln": ([D_MODEL, RT], BF16),
        "xn": ([RT, H], BF16),
        "kp": ([CC, H], BF16),
        "kfcr": ([384, H], BF16),
        "kfci": ([384, H], BF16),
        "srb": ([4*128, 257], BF16),
        "sib": ([4*128, 257], BF16),
        "cctr": ([512, 512], BF16),
        "nccti": ([512, 512], BF16),
        "dfm": ([H, RT], BF16),
    }
    for name in debug:
        dout(name, *shapes[name])

    es = ExitStack()
    t = {}
    with es:
        def T(n, s, d):
            t[n] = es.enter_context(nc.sbuf_tensor(n, s, d))
        T("H1", [128, 16384], BF16)
        T("H2", [128, 16384], BF16)
        T("pqs", [128, 2, 512], BF16)
        T("pql", [128, 2, 8, 128], BF16)
        T("c_fwdc", [128, 2, 384], BF16)
        T("c_fwds", [128, 2, 384], BF16)
        T("c_invc", [128, 3, 256], BF16)
        T("c_invs", [128, 3, 256], BF16)
        T("c_identf", [128, 128], F32)
        T("c_jrow", [128, 257], F32)
        T("tb1", [128, 257], F32)
        T("tb2", [128, 257], F32)
        T("tbs1", [128, 257], F32)
        T("tbm", [128, 257], F32)
        T("srb", [128, 4, 257], BF16)
        T("sib", [128, 4, 257], BF16)
        T("etr", [128, 2, 4, 128], BF16)
        T("eti", [128, 2, 4, 128], BF16)
        T("cctr", [128, 4, 512], BF16)
        T("nccti", [128, 4, 512], BF16)
        T("kfcr", [128, 3, 512], BF16)
        T("kfci", [128, 3, 512], BF16)
        T("lamc", [128, 16], F32)
        T("mcol", [128, 16], F32)
        T("n2sc", [128, 16], F32)
        T("epsc", [128, 4], F32)
        T("n2wt", [128, 512], BF16)
        T("n2bt", [128, 512], BF16)
        T("n2wb", [1, 512], BF16)
        T("n2bb", [1, 512], BF16)
        T("onesrow", [1, 512], BF16)
        T("ones1", [1, 128], BF16)
        T("fbrow", [1, 1024], BF16)
        T("borow", [1, 1024], BF16)
        T("c_n1w", [128, 8], F32)
        T("c_n1b", [128, 8], F32)
        T("onesbf", [128, 128], BF16)
        T("c_wdin", [128, 8, 512], BF16)
        T("xcA", [128, 8, 512], F32)
        T("xcB", [128, 8, 512], F32)
        T("bfA", [128, 8, 512], BF16)
        T("bfB", [128, 8, 512], BF16)
        T("mt", [128, 512], F32)
        T("rt_", [128, 512], F32)
        T("st_", [128, 512], F32)
        T("fscr", [128, 512], F32)
        T("xfr", [128, 3, 512], BF16)
        T("xfi", [128, 3, 512], BF16)
        T("yfi", [128, 3, 512], BF16)
        T("wxgA", [128, 8, 256], BF16)
        T("wxgB", [128, 8, 256], BF16)
        T("wgtA", [128, 4, 256], BF16)
        T("wgtB", [128, 4, 256], BF16)
        T("wouA", [128, 2, 1024], BF16)
        T("wouB", [128, 2, 1024], BF16)
        t["kp"] = t["pqs"]
        t["dbrow"] = None
        for i in range(8):
            t[f"pb{i}"] = es.enter_context(nc.psum_tensor(f"pb{i}", [128, 512], F32))

        v = {}
        v["cmat"] = t["H1"][:, 0:8192].rearrange("p (a b) -> p a b", a=8)
        v["smat"] = t["H1"][:, 8192:16384].rearrange("p (a b) -> p a b", a=8)
        v["xn"] = t["H1"][:].rearrange("p (a b) -> p a b", a=32)
        v["xgP"] = t["H1"][:].rearrange("p (a b) -> p a b", a=32)
        v["wrhi"] = t["H2"][:, 0:4096].rearrange("p (a b) -> p a b", a=4)
        v["wrlo"] = t["H2"][:, 4096:8192].rearrange("p (a b) -> p a b", a=4)
        v["wihi"] = t["H2"][:, 8192:12288].rearrange("p (a b) -> p a b", a=4)
        v["wilo"] = t["H2"][:, 12288:16384].rearrange("p (a b) -> p a b", a=4)
        v["dfm"] = t["H2"][:].rearrange("p (a b) -> p a b", a=4)
        v["psir"] = t["xcA"][:, 0:4, :]
        v["psii"] = t["xcA"][:, 4:8, :]
        v["str_"] = t["xcB"][:, 0:4, :]
        v["sti"] = t["xcB"][:, 4:8, :]
        v["strb"] = t["bfB"][:, 0:4, :]
        v["nstib"] = t["bfB"][:, 4:8, :]
        v["dtm"] = [t["mt"], t["rt_"]]

        for _e in ("dve", "act", "pool"):
            CHAIN[_e] = (es.enter_context(nc.semaphore(f"CH{_e}")), 0)
        sch = Sched(nc)
        block = es.enter_context(nc.Block())
        emit_all(sch, t, v, dram, scr, yout, dbg, set(debug))
        sch.emit(es, block)
    return nc


def sdma(sch, bufs, dst, src, name="ld"):
    if isinstance(bufs, str):
        bufs = [bufs]
    return sch.add("sp", lambda s, d=dst, r=src: s.dma_start(out=d, in_=r),
                   writes=bufs, dma=True, name=name)

def gdma(sch, bufs, dst, src, name="ldg"):
    if isinstance(bufs, str):
        bufs = [bufs]
    return sch.add("pool", lambda g, d=dst, r=src: g.dma_start(out=d, in_=r),
                   writes=bufs, dma=True, name=name)

def st_sdma(sch, reads, dst, src, name="st"):
    return sch.add("sp", lambda s, d=dst, r=src: s.dma_start(out=d, in_=r),
                   reads=reads, dma=True, name=name)

def r128(ap):
    return ap.rearrange("(a p) f -> p a f", p=128)


DBG_TAPS = {}
CHAIN = {}

def BAR(sec, eng, ins):
    sem, cnt = CHAIN[eng]
    cnt += 1
    CHAIN[eng] = (sem, cnt)
    ins.then_inc(sem, 1)
    sec.wait_ge(sem, cnt)

def emit_all(sch, t, v, dram, scr, yout, dbg, dbgset):
    DBG_TAPS.clear()
    DBG_TAPS.update(dbg)
    emit_p0_loads(sch, t, v, dram)
    emit_wprep(sch, t, v, dram, scr)
    mparts = list(mbuild_parts(sch, t, v, scr))
    per = (len(mparts) + NTCH - 1) // NTCH
    for tc in range(NTCH):
        for _ in range(per):
            if mparts:
                mparts.pop(0)()
        emit_p1_pre(sch, t, v, dram, scr, tc)
    while mparts:
        mparts.pop(0)()
    emit_tables(sch, t, v, dram)
    for tc in range(NTCH):
        emit_p1_post(sch, t, v, dram, scr, tc)
    for sq in range(BPC):
        emit_p2_seq(sch, t, v, sq)
    for tc in range(NTCH):
        emit_p3_chunk(sch, t, v, dram, scr, tc)
        emit_p4_chunk(sch, t, v, dram, scr, yout, tc)
    emit_debug(sch, t, v, scr, dbg, dbgset)
    sch.add("sp", lambda s: s.nop(), reads=[f"yout{c}" for c in range(NTCH)], name="terminal")


def emit_p0_loads(sch, t, v, dram):
    A = sch.add
    sdma(sch, "c_fwdc", t["c_fwdc"][:], r128(dram["fwdc"]))
    sdma(sch, "c_fwds", t["c_fwds"][:], r128(dram["fwds"]))
    sdma(sch, "c_invc", t["c_invc"][:], r128(dram["invc"]))
    sdma(sch, "c_invs", t["c_invs"][:], r128(dram["invs"]))
    sdma(sch, "c_identf", t["c_identf"][:], dram["identf"])
    sdma(sch, "c_jrow", t["c_jrow"][:], dram["jrow"])
    sdma(sch, "c_n1w", t["c_n1w"][:], dram["n1w"])
    sdma(sch, "c_n1b", t["c_n1b"][:], dram["n1b"])
    sdma(sch, ["cmat", "H1"], v["cmat"], r128(dram["cmat"]))
    sdma(sch, ["smat", "H1"], v["smat"], r128(dram["smat"]))
    sdma(sch, [f"md_lr{n}" for n in range(4)], t["mcol"][:, 0:4], dram["lr"], name="ld_lr")
    sdma(sch, [f"md_li{n}" for n in range(4)], t["mcol"][:, 4:8], dram["li"], name="ld_li")
    gdma(sch, "fbrow", t["fbrow"][:], dram["fbias"])
    gdma(sch, "borow", t["borow"][:], dram["bout"])
    gdma(sch, "cctr_raw", t["cctr"][:], r128(dram["crt"]))
    gdma(sch, "ccti_raw", t["nccti"][:], r128(dram["cit"]))
    gdma(sch, "c_wdin", t["c_wdin"][:], r128(dram["wdin"]))
    gdma(sch, "n2wb", t["n2wb"][:], dram["n2w"])
    gdma(sch, "n2bb", t["n2bb"][:], dram["n2b"])
    gdma(sch, "dbrow", t["wouA"][0:1, 0, 0:512], dram["dvec"])
    def ms(buf, ap, val):
        A("dve", lambda vE, a=ap, q=val: vE.memset(a, q), writes=[buf], name=f"ms_{buf}")
    ms("onesbf", t["onesbf"][:], 1.0/1024.)
    ms("onesrow", t["onesrow"][:], 1.0)
    ms("ones1", t["ones1"][:], 1.0)
    A("dve", lambda vE: vE.memset(t["epsc"][:, 0:1], EPS), writes=["epsc0"], name="ms_eps")
    A("dve", lambda vE: vE.memset(t["epsc"][:, 1:2], EPS*EPS), writes=["epsc1"], name="ms_eps2")


def emit_wprep(sch, t, v, dram, scr):
    A = sch.add
    for wi_, wname in enumerate(("wfr", "wfi")):
        for kc in range(8):
            src32a = dram[wname][128*kc:128*kc+128, 0:512]
            src32b = dram[wname][128*kc:128*kc+128, 512:1024]
            sdma(sch, "mt", t["mt"][:], src32a, name=f"wp_l{wi_}{kc}a")
            sdma(sch, "rt_", t["rt_"][:], src32b, name=f"wp_l{wi_}{kc}b")
            def hifn(vE, kc=kc):
                vE.tensor_copy(t["bfA"][:, 0, :], t["mt"][:])
                return vE.tensor_copy(t["bfA"][:, 1, :], t["rt_"][:])
            A("dve", hifn, reads=["mt", "rt_", "bfA"], writes=["bfA"], name=f"wp_hi{wi_}{kc}")
            def lofn(vE, kc=kc):
                vE.tensor_tensor(t["bfB"][:, 0, :], t["mt"][:], t["bfA"][:, 0, :], ALU.subtract)
                return vE.tensor_tensor(t["bfB"][:, 1, :], t["rt_"][:], t["bfA"][:, 1, :], ALU.subtract)
            A("dve", lofn, reads=["mt", "rt_", "bfA", "bfB"], writes=["bfB"], name=f"wp_lo{wi_}{kc}")
            hname = ("wrhi", "wihi")[wi_]; lname = ("wrlo", "wilo")[wi_]
            op = st_sdma(sch, ["bfA"], scr[hname][128*kc:128*kc+128, :],
                    t["bfA"][:, 0:2, :].rearrange("p a b -> p (a b)"), name=f"wp_sh{wi_}{kc}")
            sch.last_writer[f"hw_{hname}{kc}"] = op
            op = st_sdma(sch, ["bfB"], scr[lname][128*kc:128*kc+128, :],
                    t["bfB"][:, 0:2, :].rearrange("p a b -> p (a b)"), name=f"wp_sl{wi_}{kc}")
            sch.last_writer[f"hw_{lname}{kc}"] = op


def mbuild_parts(sch, t, v, scr):
    A = sch.add
    wv = {"wrhi": v["wrhi"], "wrlo": v["wrlo"], "wihi": v["wihi"], "wilo": v["wilo"]}

    def load_half(kh, tag):
        for nm in ("wrhi", "wrlo", "wihi", "wilo"):
            op = gdma(sch, [nm, "H2"], wv[nm],
                 r128(scr[nm])[:, 4*kh:4*kh+4, :], name=f"mb_lw_{tag}_{nm}{kh}")
            sch.ops[op].deps = sorted(set(sch.ops[op].deps)
                | {sch.last_writer[f"hw_{nm}{kc}"] for kc in range(4*kh, 4*kh+4)})

    def split_evac(pb, dsthi, dstlo, ccv, rhv, scale, tag, hkey):
        A("act", lambda sc, pbn=pb: sc.mul(t["pqs"][:, 0, :], t[pbn][:], scale),
          reads=[pb, "pqs0"], writes=["pqs0"], name=f"{tag}_evh")
        A("dve", lambda vE, pbn=pb: vE.scalar_tensor_tensor(
            t["xfr"][:, 0, :], t[pbn][:], scale, t["pqs"][:, 0, :], ALU.mult, ALU.subtract),
          reads=[pb, "pqs0", "xfr0"], writes=["xfr0"], name=f"{tag}_evl")
        op = st_sdma(sch, ["pqs0"], dsthi.rearrange("(a p) r -> p a r", p=128)[:, ccv, 512*rhv:512*rhv+512],
                t["pqs"][:, 0, :], name=f"{tag}_sth")
        sch.last_writer[f"hbm_{hkey}_hi_{ccv}_{rhv}"] = op
        op = st_sdma(sch, ["xfr0"], dstlo.rearrange("(a p) r -> p a r", p=128)[:, ccv, 512*rhv:512*rhv+512],
                t["xfr"][:, 0, :], name=f"{tag}_stl")
        sch.last_writer[f"hbm_{hkey}_lo_{ccv}_{rhv}"] = op

    def stage1_batch(groups, kind, tag):
        load_half(0, tag + "h0")
        for bi, (cc_, rh) in enumerate(groups):
            if kind == "pt":
                pb = f"pb{bi}"
                def mmfn(te, cc_=cc_, rh=rh, pbn=pb, kh=0):
                    ins = None
                    for k in range(4):
                        for wn, cn in (("wrhi", "cmat"), ("wrlo", "cmat"),
                                       ("wihi", "smat"), ("wilo", "smat")):
                            ins = te.matmul(t[pbn][:], wv[wn][:, k, 128*cc_:128*cc_+128],
                                            v[cn][:, 4*kh + k, 512*rh:512*rh+512],
                                            start=(kh == 0 and k == 0 and wn == "wrhi"), stop=False)
                    return ins
                A("pe", mmfn, reads=["wrhi", "wrlo", "wihi", "wilo", "cmat", "smat", "H2"],
                  writes=[pb], name=f"{tag}_mm0_{bi}")
            else:
                pba, pbb = f"pb{2*bi}", f"pb{2*bi+1}"
                def mmfa(te, cc_=cc_, rh=rh, pbn=pba, kh=0):
                    ins = None
                    for k in range(4):
                        for wn in ("wihi", "wilo"):
                            ins = te.matmul(t[pbn][:], wv[wn][:, k, 128*cc_:128*cc_+128],
                                            v["cmat"][:, 4*kh + k, 512*rh:512*rh+512],
                                            start=(kh == 0 and k == 0 and wn == "wihi"), stop=False)
                    return ins
                def mmfb(te, cc_=cc_, rh=rh, pbn=pbb, kh=0):
                    ins = None
                    for k in range(4):
                        for wn in ("wrhi", "wrlo"):
                            ins = te.matmul(t[pbn][:], wv[wn][:, k, 128*cc_:128*cc_+128],
                                            v["smat"][:, 4*kh + k, 512*rh:512*rh+512],
                                            start=(kh == 0 and k == 0 and wn == "wrhi"), stop=False)
                    return ins
                A("pe", mmfa, reads=["wihi", "wilo", "cmat", "H2"], writes=[pba], name=f"{tag}_mma0_{bi}")
                A("pe", mmfb, reads=["wrhi", "wrlo", "smat", "H2"], writes=[pbb], name=f"{tag}_mmb0_{bi}")
        load_half(1, tag + "h1")
        for bi, (cc_, rh) in enumerate(groups):
            if kind == "pt":
                pb = f"pb{bi}"
                def mmfn(te, cc_=cc_, rh=rh, pbn=pb, kh=1):
                    ins = None
                    for k in range(4):
                        for wn, cn in (("wrhi", "cmat"), ("wrlo", "cmat"),
                                       ("wihi", "smat"), ("wilo", "smat")):
                            last = (k == 3 and wn == "wilo")
                            ins = te.matmul(t[pbn][:], wv[wn][:, k, 128*cc_:128*cc_+128],
                                            v[cn][:, 4*kh + k, 512*rh:512*rh+512],
                                            start=False, stop=last)
                    return ins
                A("pe", mmfn, reads=["wrhi", "wrlo", "wihi", "wilo", "cmat", "smat", "H2", pb],
                  writes=[pb], name=f"{tag}_mm1_{bi}")
                split_evac(pb, scr["pt_hi"], scr["pt_lo"], cc_, rh, 1.0, f"{tag}_e{bi}", "pt")
            else:
                pba, pbb = f"pb{2*bi}", f"pb{2*bi+1}"
                def mmfa(te, cc_=cc_, rh=rh, pbn=pba, kh=1):
                    ins = None
                    for k in range(4):
                        for wn in ("wihi", "wilo"):
                            ins = te.matmul(t[pbn][:], wv[wn][:, k, 128*cc_:128*cc_+128],
                                            v["cmat"][:, 4*kh + k, 512*rh:512*rh+512],
                                            start=False, stop=(k == 3 and wn == "wilo"))
                    return ins
                def mmfb(te, cc_=cc_, rh=rh, pbn=pbb, kh=1):
                    ins = None
                    for k in range(4):
                        for wn in ("wrhi", "wrlo"):
                            ins = te.matmul(t[pbn][:], wv[wn][:, k, 128*cc_:128*cc_+128],
                                            v["smat"][:, 4*kh + k, 512*rh:512*rh+512],
                                            start=False, stop=(k == 3 and wn == "wrlo"))
                    return ins
                A("pe", mmfa, reads=["wihi", "wilo", "cmat", "H2", pba], writes=[pba],
                  name=f"{tag}_mma1_{bi}")
                A("pe", mmfb, reads=["wrhi", "wrlo", "smat", "H2", pbb], writes=[pbb],
                  name=f"{tag}_mmb1_{bi}")
                A("act", lambda sc, pbn=pbb: sc.copy(t["fscr"][:], t[pbn][:]),
                  reads=[pbb, "fscr"], writes=["fscr"], name=f"{tag}_cp{bi}")
                A("dve", lambda vE, pbn=pba:
                    vE.tensor_tensor(t["fscr"][:], t["fscr"][:], t[pbn][:], ALU.subtract),
                  reads=[pba, "fscr"], writes=["fscr"], name=f"{tag}_sub{bi}")
                A("act", lambda sc: sc.copy(t["pqs"][:, 0, :], t["fscr"][:]),
                  reads=["fscr", "pqs0"], writes=["pqs0"], name=f"{tag}_hi{bi}")
                A("dve", lambda vE:
                    vE.tensor_tensor(t["xfr"][:, 0, :], t["fscr"][:], t["pqs"][:, 0, :], ALU.subtract),
                  reads=["fscr", "pqs0", "xfr0"], writes=["xfr0"], name=f"{tag}_lo{bi}")
                cc_g, rh_g = groups[bi]
                op = st_sdma(sch, ["pqs0"],
                        scr["qt_hi"].rearrange("(a p) r -> p a r", p=128)[:, cc_g, 512*rh_g:512*rh_g+512],
                        t["pqs"][:, 0, :], name=f"{tag}_sth{bi}")
                sch.last_writer[f"hbm_qt_hi_{cc_g}_{rh_g}"] = op
                op = st_sdma(sch, ["xfr0"],
                        scr["qt_lo"].rearrange("(a p) r -> p a r", p=128)[:, cc_g, 512*rh_g:512*rh_g+512],
                        t["xfr"][:, 0, :], name=f"{tag}_stl{bi}")
                sch.last_writer[f"hbm_qt_lo_{cc_g}_{rh_g}"] = op
        return

    groups = [(cc, rh) for cc in range(8) for rh in range(2)]
    for b in range(2):
        yield (lambda b=b: stage1_batch(groups[8*b:8*b+8], "pt", f"pt{b}"))
    for b in range(4):
        yield (lambda b=b: stage1_batch(groups[4*b:4*b+4], "qt", f"qt{b}"))
    def m_part(r):
        for rnd, (ph, qh) in enumerate((("pt_hi", "qt_hi"), ("pt_lo", "qt_lo"))):
            hp = "hi" if rnd == 0 else "lo"
            op = sdma(sch, "pql0", t["pql"][:, 0, :, :],
                 r128(scr[ph])[:, :, 128*r:128*r+128], name=f"m_ldp{r}_{rnd}")
            sch.ops[op].deps = sorted(set(sch.ops[op].deps)
                | {sch.last_writer[f"hbm_pt_{hp}_{cc}_{rh}"] for cc in range(8) for rh in range(2)})
            op = sdma(sch, "pql1", t["pql"][:, 1, :, :],
                 r128(scr[qh])[:, :, 128*r:128*r+128], name=f"m_ldq{r}_{rnd}")
            sch.ops[op].deps = sorted(set(sch.ops[op].deps)
                | {sch.last_writer[f"hbm_qt_{hp}_{cc}_{rh}"] for cc in range(8) for rh in range(2)})
            for oh in range(2):
                pb = f"pb{(r*2+oh) % 8}"
                def mfn(te, r=r, oh=oh, pbn=pb, rnd=rnd):
                    ins = None
                    for cc2 in range(8):
                        te.matmul(t[pbn][:], t["pql"][:, 0, cc2, :],
                                  v["cmat"][:, cc2, 512*oh:512*oh+512],
                                  start=(rnd == 0 and cc2 == 0), stop=False)
                        ins = te.matmul(t[pbn][:], t["pql"][:, 1, cc2, :],
                                        v["smat"][:, cc2, 512*oh:512*oh+512],
                                        start=False, stop=(rnd == 1 and cc2 == 7))
                    return ins
                A("pe", mfn, reads=["pql0", "pql1", "cmat", "smat"] + ([pb] if rnd else []),
                  writes=[pb] if rnd == 0 else [], name=f"m_mm{r}_{oh}_{rnd}")
        for oh in range(2):
            pb = f"pb{(r*2+oh) % 8}"
            split_evac(pb, scr["m_hi"], scr["m_lo"], r, oh, 1.0/1024., f"m_e{r}_{oh}", "m")
    for r in range(8):
        yield (lambda r=r: m_part(r))


def emit_tables(sch, t, v, dram):
    A = sch.add
    jr = t["c_jrow"][:]
    for n in range(4):
        a_ap = t["lamc"][:, 12+n:13+n]
        b_ap = t["mcol"][:, 8+n:9+n]
        A("act", lambda sc, o=a_ap, i=t["mcol"][:, n:n+1]: sc.activation(o, i, AF.Exp),
          reads=[f"md_lr{n}"], writes=[f"a{n}"], name=f"tb_a{n}")
        A("act", lambda sc, o=b_ap, i=t["mcol"][:, 4+n:5+n]: sc.activation(o, i, AF.Exp),
          reads=[f"md_li{n}"], writes=[f"b{n}"], name=f"tb_b{n}")
        c_ap = t["mcol"][:, 12+n:13+n]
        chi = t["n2sc"][:, 8+n:9+n]
        clo = t["n2sc"][:, 12+n:13+n]
        def csplit(vE, b=b_ap, c=c_ap, chi=chi, clo=clo):
            BAR(vE, "dve", vE.tensor_scalar_mul(c, b, INV2PI))
            BAR(vE, "dve", vE.tensor_scalar(chi, c, 4096.0, MAGIC, ALU.mult, ALU.add))
            BAR(vE, "dve", vE.tensor_scalar_add(chi, chi, -MAGIC))
            BAR(vE, "dve", vE.tensor_scalar_mul(chi, chi, 1.0/4096.0))
            return vE.tensor_tensor(clo, c, chi, ALU.subtract)
        A("dve", csplit, reads=[f"b{n}"], writes=[f"chi{n}"], name=f"tb_csplit{n}")
        def fracA(vE, chi=chi, clo=clo):
            BAR(vE, "dve", vE.tensor_scalar(t["tb1"][:], jr, chi, None, ALU.mult))
            BAR(vE, "dve", vE.tensor_scalar_add(t["tb2"][:], t["tb1"][:], MAGIC))
            BAR(vE, "dve", vE.tensor_scalar_add(t["tb2"][:], t["tb2"][:], -MAGIC))
            BAR(vE, "dve", vE.tensor_tensor(t["tb1"][:], t["tb1"][:], t["tb2"][:], ALU.subtract))
            BAR(vE, "dve", vE.tensor_scalar(t["tb2"][:], jr, clo, None, ALU.mult))
            BAR(vE, "dve", vE.tensor_tensor(t["tb1"][:], t["tb1"][:], t["tb2"][:], ALU.add))
            BAR(vE, "dve", vE.tensor_scalar_add(t["tb2"][:], t["tb1"][:], MAGIC))
            BAR(vE, "dve", vE.tensor_scalar_add(t["tb2"][:], t["tb2"][:], -MAGIC))
            return vE.tensor_tensor(t["tb2"][:], t["tb1"][:], t["tb2"][:], ALU.subtract)
        A("dve", fracA, reads=[f"chi{n}", "c_jrow", "tb1", "tb2"],
          writes=["tb1", "tb2"], name=f"tb_fracA{n}")
        A("act", lambda sc: sc.activation(t["tbs1"][:], t["tb2"][:], AF.Sin, scale=TWO_PI),
          reads=["tb2", "tbs1"], writes=["tbs1"], name=f"tb_sin{n}")
        def fracB(vE):
            BAR(vE, "dve", vE.tensor_scalar_add(t["tb1"][:], t["tb1"][:], 0.25))
            BAR(vE, "dve", vE.tensor_scalar_add(t["tb2"][:], t["tb1"][:], MAGIC))
            BAR(vE, "dve", vE.tensor_scalar_add(t["tb2"][:], t["tb2"][:], -MAGIC))
            return vE.tensor_tensor(t["tb1"][:], t["tb1"][:], t["tb2"][:], ALU.subtract)
        A("dve", fracB, reads=["tb1", "tb2"], writes=["tb1", "tb2"],
          name=f"tb_fracB{n}")
        A("act", lambda sc: sc.activation(t["tb2"][:], t["tb1"][:], AF.Sin, scale=TWO_PI),
          reads=["tb1", "tb2"], writes=["tb2"], name=f"tb_cos{n}")
        na = t["n2sc"][:, 8+n:9+n]
        A("dve", lambda vE, na=na, a=a_ap: vE.tensor_scalar_mul(na, a, -1.0),
          reads=[f"a{n}", f"chi{n}"], writes=[f"na{n}"], name=f"tb_na{n}")
        A("act", lambda sc, na=na: sc.activation(t["tbm"][:], jr, AF.Exp, scale=na),
          reads=[f"na{n}", "c_jrow", "tbm"], writes=["tbm"], name=f"tb_mag{n}")
        A("dve", lambda vE: vE.tensor_tensor(t["tb1"][:], t["tbm"][:], t["tb2"][:], ALU.mult),
          reads=["tbm", "tb2", "tb1"], writes=["tb1"], name=f"tb_srf{n}")
        A("dve", lambda vE: vE.tensor_tensor(t["tbs1"][:], t["tbm"][:], t["tbs1"][:], ALU.mult),
          reads=["tbm", "tbs1"], writes=["tbs1"], name=f"tb_sif{n}")
        A("dve", lambda vE, n=n: vE.tensor_copy(t["srb"][:, n, :], t["tb1"][:]),
          reads=["tb1"], writes=[f"srb{n}"], name=f"tb_srb{n}")
        A("dve", lambda vE, n=n: vE.tensor_copy(t["sib"][:, n, :], t["tbs1"][:]),
          reads=["tbs1"], writes=[f"sib{n}"], name=f"tb_sib{n}")
        def lamcols(vE, n=n):
            vE.tensor_copy(t["lamc"][:, n:n+1], t["tb1"][:, 256:257])
            vE.tensor_copy(t["lamc"][:, 4+n:5+n], t["tbs1"][:, 256:257])
            return vE.tensor_scalar_mul(t["lamc"][:, 8+n:9+n], t["tbs1"][:, 256:257], -1.0)
        A("dve", lamcols, reads=["tb1", "tbs1"], writes=[f"lamc{n}"], name=f"tb_lamc{n}")
        for mch in range(2):
            hi = CC - 128*mch
            for im in range(2):
                srcb = "tb1" if im == 0 else "tbs1"
                dst = "etr" if im == 0 else "eti"
                A("dve", lambda vE, hi=hi, srcb=srcb:
                    vE.tensor_copy(t["fscr"][:, 0:128], t[srcb][:, hi-128:hi][:, ::-1]),
                  reads=[srcb, "fscr"], writes=["fscr"], name=f"et_rv{n}{mch}{im}")
                def trf(te):
                    return te.transpose(t["pb5"][:, 0:128], t["fscr"][:, 0:128], t["c_identf"][:])
                A("pe", trf, reads=["fscr", "c_identf"], writes=["pb5"], name=f"et_tr{n}{mch}{im}")
                def evf(sc, n=n, mch=mch, dst=dst):
                    return sc.copy(t[dst][:, mch, n, :], t["pb5"][:, 0:128])
                A("act", evf, reads=["pb5"], writes=[f"{dst}{n}{mch}"], name=f"et_ev{n}{mch}{im}")
        def wpipe(vE, n=n):
            ur = t["n2sc"][:, 12+n:13+n]
            ui = t["mcol"][:, 12+n:13+n]
            vE.tensor_scalar_add(ur, t["tb1"][:, 1:2], -1.0)
            BAR(vE, "dve", vE.tensor_copy(ui, t["tbs1"][:, 1:2]))
            a = t["lamc"][:, 12+n:13+n]
            b = t["mcol"][:, 8+n:9+n]
            dn = t["tb2"][:, 0:1]
            vE.tensor_tensor(dn, a, a, ALU.mult)
            BAR(vE, "dve", vE.tensor_tensor(t["tb2"][:, 1:2], b, b, ALU.mult))
            BAR(vE, "dve", vE.tensor_tensor(dn, dn, t["tb2"][:, 1:2], ALU.add))
            BAR(vE, "dve", vE.reciprocal(dn, dn))
            wr = t["tb2"][:, 2:3]; s_ = t["tb2"][:, 3:4]
            vE.tensor_tensor(wr, ui, b, ALU.mult)
            BAR(vE, "dve", vE.tensor_tensor(t["tb2"][:, 4:5], ur, a, ALU.mult))
            BAR(vE, "dve", vE.tensor_tensor(wr, wr, t["tb2"][:, 4:5], ALU.subtract))
            BAR(vE, "dve", vE.tensor_tensor(wr, wr, dn, ALU.mult))
            vE.tensor_tensor(s_, ui, a, ALU.mult)
            BAR(vE, "dve", vE.tensor_tensor(t["tb2"][:, 4:5], ur, b, ALU.mult))
            BAR(vE, "dve", vE.tensor_tensor(s_, s_, t["tb2"][:, 4:5], ALU.add))
            BAR(vE, "dve", vE.tensor_tensor(s_, s_, dn, ALU.mult))
            c_raw = t["cctr"][:, n, :]; i_raw = t["nccti"][:, n, :]
            t1 = t["kfcr"][:, 0, :]; t2 = t["kfcr"][:, 1, :]
            vE.tensor_scalar(t1, c_raw, wr, None, ALU.mult)
            BAR(vE, "dve", vE.tensor_scalar(t2, i_raw, s_, None, ALU.mult))
            vE.tensor_scalar(t["st_"][:], c_raw, s_, None, ALU.mult)
            BAR(vE, "dve", vE.tensor_scalar(t["fscr"][:], i_raw, wr, None, ALU.mult))
            BAR(vE, "dve", vE.tensor_tensor(t["cctr"][:, n, :], t1, t2, ALU.add))
            return vE.tensor_tensor(t["nccti"][:, n, :], t["st_"][:], t["fscr"][:], ALU.subtract)
        A("dve", wpipe,
          reads=["tb1", "tbs1", f"a{n}", f"b{n}", "cctr_raw", "ccti_raw", "tb2",
                 "kfcr0", "kfcr1", "st_", "fscr"],
          writes=[f"cct{n}", "kfcr0", "kfcr1", "st_", "fscr", "tb2"], name=f"tb_w{n}")
    emit_tables_k(sch, t)
    A("pe", lambda te: te.matmul(t["pb6"][:], t["ones1"][:], t["n2wb"][:], start=True, stop=True),
      reads=["ones1", "n2wb"], writes=["pb6"], name="n2w_mm")
    A("act", lambda sc: sc.copy(t["n2wt"][:], t["pb6"][:]), reads=["pb6"], writes=["n2wt"],
      name="n2w_ev")
    A("pe", lambda te: te.matmul(t["pb6"][:], t["ones1"][:], t["n2bb"][:], start=True, stop=True),
      reads=["ones1", "n2bb"], writes=["pb6"], name="n2b_mm")
    A("act", lambda sc: sc.copy(t["n2bt"][:], t["pb6"][:]), reads=["pb6"], writes=["n2bt"],
      name="n2b_ev")


def emit_tables_k(sch, t):
    A = sch.add
    for jc in range(2):
        pb = f"pb{jc}"
        def kmm(te, jc=jc, pbn=pb):
            ins = None
            for n in range(4):
                te.matmul(t[pbn][:], t["srb"][:, n, 128*jc:128*jc+128], t["cctr"][:, n, :],
                          start=(n == 0), stop=False)
                ins = te.matmul(t[pbn][:], t["sib"][:, n, 128*jc:128*jc+128], t["nccti"][:, n, :],
                                start=False, stop=(n == 3))
            return ins
        A("pe", kmm, reads=[f"srb{n}" for n in range(4)] + [f"sib{n}" for n in range(4)]
          + [f"cct{n}" for n in range(4)], writes=[pb], name=f"k_mm{jc}")
        A("act", lambda sc, jc=jc, pbn=pb: sc.copy(t["kp"][:, jc, :], t[pbn][:]),
          reads=[pb], writes=[f"kp{jc}", f"pqs{jc}"], name=f"k_ev{jc}")
    A("dve", lambda vE: vE.tensor_tensor(t["kp"][0:1, 0, :], t["kp"][0:1, 0, :],
                                         t["wouA"][0:1, 0, 0:512], ALU.add),
      reads=["kp0", "dbrow", "wou0"], writes=["kp0", "pqs0"], name="k_dtap")
    for fc in range(3):
        for im in range(2):
            mat = "c_fwdc" if im == 0 else "c_fwds"
            dst = "kfcr" if im == 0 else "kfci"
            pb = f"pb{(2 + fc*2 + im) % 8}"
            def kfmm(te, fc=fc, mat=mat, pbn=pb):
                ins = None
                for mch in range(2):
                    ins = te.matmul(t[pbn][:], t[mat][:, mch, 128*fc:128*fc+128],
                                    t["kp"][:, mch, :], start=(mch == 0), stop=(mch == 1))
                return ins
            A("pe", kfmm, reads=["kp0", "kp1", mat], writes=[pb], name=f"kf_mm{fc}{im}")
            wr_extra = [f"{dst}{fc}"]
            if dst == "kfcr" and fc in (0, 1):
                wr_extra.append(f"kfcr{fc}_scr")
            A("act", lambda sc, fc=fc, dst=dst, pbn=pb: sc.copy(t[dst][:, fc, :], t[pbn][:]),
              reads=[pb] + (["kfcr0", "kfcr1"] if dst == "kfcr" and fc < 2 else []),
              writes=wr_extra + ([f"{dst}{fc}"] if False else []), name=f"kf_ev{fc}{im}")


def emit_p1_pre(sch, t, v, dram, scr, tc):
    A = sch.add
    xc = t["xcA"] if tc % 2 == 0 else t["xcB"]
    xcb = f"xc{tc % 2}"
    sdma(sch, xcb, xc[:], r128(dram["x0"])[:, :, 512*tc:512*tc+512], name=f"p1a_ld{tc}")
    A("dve", lambda vE, xc=xc: vE.tensor_copy(t["bfA"][:], xc[:]),
      reads=[xcb], writes=["bfA"], name=f"p1a_cast{tc}")
    A("act", lambda sc, xc=xc: sc.activation(t["bfB"][:], xc[:], AF.Square),
      reads=[xcb], writes=["bfB"], name=f"p1a_sq{tc}")
    def stats1(te):
        ins = None
        for k in range(8):
            ins = te.matmul(t["pb6"][:], t["onesbf"][:], t["bfA"][:, k, :],
                            start=(k == 0), stop=(k == 7))
        return ins
    A("pe", stats1, reads=["bfA", "onesbf"], writes=["pb6"], name=f"p1a_s1_{tc}")
    def stats2(te):
        ins = None
        for k in range(8):
            ins = te.matmul(t["pb7"][:], t["onesbf"][:], t["bfB"][:, k, :],
                            start=(k == 0), stop=(k == 7))
        return ins
    A("pe", stats2, reads=["bfB", "onesbf"], writes=["pb7"], name=f"p1a_s2_{tc}")
    A("act", lambda sc: sc.copy(t["mt"][:], t["pb6"][:]), reads=["pb6"], writes=["mt"],
      name=f"p1a_m{tc}")
    def varfn(vE):
        BAR(vE, "dve", vE.tensor_tensor(t["rt_"][:], t["mt"][:], t["mt"][:], ALU.mult))
        return vE.tensor_tensor(t["rt_"][:], t["pb7"][:], t["rt_"][:], ALU.subtract)
    A("dve", varfn, reads=["mt", "pb7", "rt_"], writes=["rt_"], name=f"p1a_var{tc}")
    def rfn(sc):
        BAR(sc, "act", sc.activation(t["st_"][:], t["rt_"][:], AF.Ln, bias=t["epsc"][:, 0:1]))
        return sc.activation(t["rt_"][:], t["st_"][:], AF.Exp, scale=-0.5)
    A("act", rfn, reads=["rt_", "epsc0", "st_"], writes=["rt_", "st_"], name=f"p1a_r{tc}")
    def ufn(vE, xc=xc):
        ins = None
        for k in range(8):
            BAR(vE, "dve", vE.tensor_tensor(t["bfB"][:, k, :], xc[:, k, :], t["mt"][:], ALU.subtract))
            ins = vE.tensor_tensor(t["bfA"][:, k, :], t["bfB"][:, k, :], t["rt_"][:], ALU.mult)
        return ins
    A("dve", ufn, reads=[xcb, "mt", "rt_"], writes=["bfA", "bfB"], name=f"p1a_u{tc}")
    op = st_sdma(sch, ["bfA"], r128(scr["u1"])[:, :, 512*tc:512*tc+512], t["bfA"][:], name=f"p1a_sp{tc}")
    sch.last_writer[f"hu1_{tc}"] = op


def emit_p1_post(sch, t, v, dram, scr, tc):
    A = sch.add
    xc = t["xcA"] if tc % 2 == 0 else t["xcB"]
    xcb = f"xc{tc % 2}"
    sdma(sch, xcb, xc[:], r128(dram["x0"])[:, :, 512*tc:512*tc+512], name=f"p1b_ldx{tc}")
    sdma(sch, "bfB", t["bfB"][:], r128(scr["u1"])[:, :, 512*tc:512*tc+512], name=f"p1b_ldu{tc}")
    sch.ops[-1].deps = sorted(set(sch.ops[-1].deps) | {sch.last_writer[f"hu1_{tc}"]})
    for phase, mkey in enumerate(("m_hi", "m_lo")):
        for oc in range(8):
            mb = f"pql{oc % 2}"
            op = gdma(sch, mb, t["pql"][:, oc % 2, :, :],
                 r128(scr[mkey])[:, :, 128*oc:128*oc+128], name=f"p1b_ldm{tc}_{phase}_{oc}")
            hp = "hi" if phase == 0 else "lo"
            sch.ops[op].deps = sorted(set(sch.ops[op].deps)
                | {sch.last_writer[f"hbm_m_{hp}_{r}_{h}"] for r in range(8) for h in range(2)})
            pb = f"pb{oc}"
            def ffn_mm(te, oc=oc, pbn=pb, phase=phase):
                ins = None
                for k in range(8):
                    ins = te.matmul(t[pbn][:], t["pql"][:, oc % 2, k, :], t["bfB"][:, k, :],
                                    start=(phase == 0 and k == 0), stop=False)
                if phase == 1:
                    ins = te.matmul(t[pbn][:], t["fbrow"][0:1, 128*oc:128*oc+128],
                                    t["onesrow"][:], start=False, stop=True)
                return ins
            A("pe", ffn_mm, reads=["bfB", "fbrow", "onesrow", mb] + ([pb] if phase else []),
              writes=[pb], name=f"p1b_ffn{tc}_{phase}_{oc}")
    for oc in range(8):
        pb = f"pb{oc}"
        A("dve", lambda vE, oc=oc, pbn=pb, xc=xc:
            vE.scalar_tensor_tensor(xc[:, oc, :], t[pbn][:], 0.5, xc[:, oc, :], ALU.mult, ALU.add),
          reads=[pb, xcb], writes=[xcb], name=f"p1b_x1_{tc}_{oc}")
    st_sdma(sch, [xcb], r128(scr["x1raw"])[:, :, 512*tc:512*tc+512], xc[:], name=f"p1b_spraw{tc}")
    A("dve", lambda vE, xc=xc: vE.tensor_copy(t["bfA"][:], xc[:]),
      reads=[xcb], writes=["bfA"], name=f"p1b_cast{tc}")
    A("act", lambda sc, xc=xc: sc.activation(t["bfB"][:], xc[:], AF.Square),
      reads=[xcb, "bfB"], writes=["bfB"], name=f"p1b_sq{tc}")
    def stats1(te):
        ins = None
        for k in range(8):
            ins = te.matmul(t["pb6"][:], t["onesbf"][:], t["bfA"][:, k, :],
                            start=(k == 0), stop=(k == 7))
        return ins
    A("pe", stats1, reads=["bfA", "onesbf"], writes=["pb6"], name=f"p1b_s1_{tc}")
    def stats2(te):
        ins = None
        for k in range(8):
            ins = te.matmul(t["pb7"][:], t["onesbf"][:], t["bfB"][:, k, :],
                            start=(k == 0), stop=(k == 7))
        return ins
    A("pe", stats2, reads=["bfB", "onesbf"], writes=["pb7"], name=f"p1b_s2_{tc}")
    A("act", lambda sc: sc.copy(t["mt"][:], t["pb6"][:]), reads=["pb6"], writes=["mt"],
      name=f"p1b_m{tc}")
    def varfn(vE):
        BAR(vE, "dve", vE.tensor_tensor(t["rt_"][:], t["mt"][:], t["mt"][:], ALU.mult))
        return vE.tensor_tensor(t["fscr"][:], t["pb7"][:], t["rt_"][:], ALU.subtract)
    A("dve", varfn, reads=["mt", "pb7", "rt_", "fscr"], writes=["fscr", "rt_"],
      name=f"p1b_var{tc}")
    def rsfn(sc):
        BAR(sc, "act", sc.activation(t["st_"][:], t["fscr"][:], AF.Ln, bias=t["epsc"][:, 0:1]))
        BAR(sc, "act", sc.activation(t["rt_"][:], t["st_"][:], AF.Exp, scale=-0.5))
        BAR(sc, "act", sc.activation(t["st_"][:], t["fscr"][:], AF.Ln, bias=t["epsc"][:, 1:2], scale=1.0+EPS))
        return sc.activation(t["st_"][:], t["st_"][:], AF.Exp, scale=-0.5)
    A("act", rsfn, reads=["fscr", "epsc0", "epsc1", "st_", "rt_"], writes=["rt_", "st_"],
      name=f"p1b_rs{tc}")
    def vfn(vE, xc=xc):
        ins = None
        for k in range(8):
            BAR(vE, "dve", vE.tensor_tensor(t["fscr"][:], xc[:, k, :], t["mt"][:], ALU.subtract))
            BAR(vE, "dve", vE.tensor_tensor(t["bfA"][:, k, :], t["fscr"][:], t["st_"][:], ALU.mult))
            vE.tensor_scalar(t["bfA"][:, k, :], t["bfA"][:, k, :], t["c_n1w"][:, k:k+1],
                             t["c_n1b"][:, k:k+1], ALU.mult, ALU.add)
            BAR(vE, "dve", vE.tensor_tensor(t["fscr"][:], t["fscr"][:], t["rt_"][:], ALU.mult))
            ins = vE.tensor_tensor(xc[:, k, :], xc[:, k, :], t["fscr"][:], ALU.add)
        return ins
    A("dve", vfn, reads=[xcb, "mt", "st_", "rt_", "c_n1w", "c_n1b", "fscr"],
      writes=[xcb, "bfA", "fscr"], name=f"p1b_v{tc}")
    op = st_sdma(sch, [xcb], r128(scr["x1v"])[:, :, 512*tc:512*tc+512], xc[:], name=f"p1b_spx{tc}")
    sch.last_writer[f"hx1v_{tc}"] = op
    op = st_sdma(sch, ["bfA"], r128(scr["xln"])[:, :, 512*tc:512*tc+512], t["bfA"][:], name=f"p1b_spl{tc}")
    sch.last_writer[f"hxln_{tc}"] = op
    for tt in range(4):
        pb = f"pb{tt}"
        def xd_mm(te, tt=tt, pbn=pb):
            ins = None
            for k in range(8):
                ins = te.matmul(t[pbn][:], t["bfA"][:, k, 128*tt:128*tt+128], t["c_wdin"][:, k, :],
                                start=(k == 0), stop=(k == 7))
            return ins
        A("pe", xd_mm, reads=["bfA", "c_wdin"], writes=[pb], name=f"p1b_xd{tc}_{tt}")
        slot = 4*tc + tt
        def n2fn(vE, tt=tt, pbn=pb):
            m_ = t["n2sc"][:, tt:tt+1]
            q_ = t["n2sc"][:, 4+tt:5+tt]
            BAR(vE, "dve", vE.tensor_copy(t["mt"][:], t[pbn][:]))
            BAR(vE, "dve", vE.reduce_sum(m_, t["mt"][:], axis=AX.X))
            BAR(vE, "dve", vE.tensor_scalar_mul(m_, m_, 1.0/512.0))
            BAR(vE, "dve", vE.tensor_tensor(t["fscr"][:], t["mt"][:], t["mt"][:], ALU.mult))
            BAR(vE, "dve", vE.reduce_sum(q_, t["fscr"][:], axis=AX.X))
            BAR(vE, "dve", vE.tensor_scalar_mul(q_, q_, 1.0/512.0))
            BAR(vE, "dve", vE.tensor_tensor(t["n2sc"][:, 8+tt:9+tt], m_, m_, ALU.mult))
            return vE.tensor_tensor(q_, q_, t["n2sc"][:, 8+tt:9+tt], ALU.subtract)
        A("dve", n2fn, reads=[pb, "mt", "fscr"], writes=["mt", "fscr", f"n2v{tt}"],
          name=f"p1b_n2v{tc}_{tt}")
        def n2rfn(sc, tt=tt):
            q_ = t["n2sc"][:, 4+tt:5+tt]
            BAR(sc, "act", sc.activation(q_, q_, AF.Ln, bias=t["epsc"][:, 0:1]))
            return sc.activation(q_, q_, AF.Exp, scale=-0.5)
        A("act", n2rfn, reads=[f"n2v{tt}", "epsc0"], writes=[f"n2r{tt}"], name=f"p1b_n2r{tc}_{tt}")
        def xnfn(vE, tt=tt, slot=slot):
            m_ = t["n2sc"][:, tt:tt+1]
            q_ = t["n2sc"][:, 4+tt:5+tt]
            xns = v["xn"][:, slot, :]
            BAR(vE, "dve", vE.tensor_scalar(t["fscr"][:], t["mt"][:], m_, q_, ALU.subtract, ALU.mult))
            BAR(vE, "dve", vE.tensor_tensor(t["fscr"][:], t["fscr"][:], t["n2wt"][:], ALU.mult))
            return vE.tensor_tensor(xns, t["fscr"][:], t["n2bt"][:], ALU.add)
        A("dve", xnfn, reads=["mt", f"n2r{tt}", "n2wt", "n2bt", "fscr"],
          writes=[f"xn{slot}", "fscr", "H1"], name=f"p1b_xn{tc}_{tt}")


def emit_p2_seq(sch, t, v, sq):
    A = sch.add
    A("dve", lambda vE: vE.memset(t["xcB"][:], 0.0), writes=["xc1", "str_", "sti"],
      name=f"p2_rst{sq}")
    A("dve", lambda vE: vE.memset(t["bfB"][:], 0.0), writes=["bfB", "strb", "nstib"],
      name=f"p2_rstb{sq}")
    for i in range(NCH):
        slots = (16*sq + 2*i, 16*sq + 2*i + 1)
        xnslots = [f"xn{s}" for s in slots]
        for fc in range(3):
            for im in range(2):
                mat = "c_fwdc" if im == 0 else "c_fwds"
                pb = f"pb{fc*2+im}"
                def fmm(te, fc=fc, mat=mat, pbn=pb, slots=slots):
                    ins = None
                    for mch in range(2):
                        ins = te.matmul(t[pbn][:], t[mat][:, mch, 128*fc:128*fc+128],
                                        v["xn"][:, slots[mch], :], start=(mch == 0), stop=(mch == 1))
                    return ins
                A("pe", fmm, reads=xnslots + [mat], writes=[pb], name=f"p2f{sq}_{i}_{fc}{im}")
                dst = "xfr" if im == 0 else "xfi"
                A("act", lambda sc, fc=fc, dst=dst, pbn=pb: sc.copy(t[dst][:, fc, :], t[pbn][:]),
                  reads=[pb], writes=[f"{dst}{fc}"], name=f"p2fe{sq}_{i}_{fc}{im}")
        for fc in range(3):
            eng = "dve" if fc % 2 == 0 else "pool"
            def prodI(vE, fc=fc, eng=eng):
                vE.tensor_tensor(t["yfi"][:, fc, :], t["xfr"][:, fc, :], t["kfci"][:, fc, :], ALU.mult)
                BAR(vE, eng, vE.tensor_tensor(t["st_"][:], t["xfi"][:, fc, :], t["kfcr"][:, fc, :], ALU.mult))
                return vE.tensor_tensor(t["yfi"][:, fc, :], t["yfi"][:, fc, :], t["st_"][:], ALU.add)
            A(eng, prodI, reads=[f"xfr{fc}", f"xfi{fc}", f"kfcr{fc}", f"kfci{fc}", "st_"],
              writes=[f"yfi{fc}", "st_"], name=f"p2pI{sq}_{i}_{fc}")
            def prodR(vE, fc=fc, eng=eng):
                vE.tensor_tensor(t["st_"][:], t["xfi"][:, fc, :], t["kfci"][:, fc, :], ALU.mult)
                BAR(vE, eng, vE.tensor_tensor(t["xfr"][:, fc, :], t["xfr"][:, fc, :], t["kfcr"][:, fc, :], ALU.mult))
                return vE.tensor_tensor(t["xfr"][:, fc, :], t["xfr"][:, fc, :], t["st_"][:], ALU.subtract)
            A(eng, prodR, reads=[f"xfr{fc}", f"xfi{fc}", f"kfcr{fc}", f"kfci{fc}", "st_"],
              writes=[f"yfr{fc}", f"xfr{fc}", "st_"], name=f"p2pR{sq}_{i}_{fc}")
        for n in range(4):
            for im in range(2):
                mat = "etr" if im == 0 else "eti"
                pb = f"pb{n*2+im}"
                def emm(te, n=n, mat=mat, pbn=pb, slots=slots):
                    ins = None
                    for mch in range(2):
                        ins = te.matmul(t[pbn][:], t[mat][:, mch, n, :],
                                        v["xn"][:, slots[mch], :], start=(mch == 0), stop=(mch == 1))
                    return ins
                A("pe", emm, reads=xnslots + [f"{mat}{n}0", f"{mat}{n}1"], writes=[pb],
                  name=f"p2e{sq}_{i}_{n}{im}")
                def pev(sc, n=n, im=im, pbn=pb):
                    return sc.copy(t["bfA"][:, 4*im + n, :], t[pbn][:])
                A("act", pev, reads=[pb], writes=[f"ph{im}{n}", "bfA"], name=f"p2pe{sq}_{i}_{n}{im}")
        for n in range(4):
            eng = "dve" if n % 2 == 0 else "pool"
            def foldR(vE, n=n, eng=eng):
                vE.tensor_tensor(v["psir"][:, n, :], t["cctr"][:, n, :], t["bfA"][:, n, :], ALU.mult)
                BAR(vE, eng, vE.tensor_tensor(t["fscr"][:], t["nccti"][:, n, :], t["bfA"][:, 4+n, :], ALU.mult))
                return vE.tensor_tensor(v["psir"][:, n, :], v["psir"][:, n, :], t["fscr"][:], ALU.add)
            A(eng, foldR, reads=[f"ph0{n}", f"ph1{n}", f"cct{n}", "fscr"],
              writes=[f"psir{n}", "fscr", "xc0"], name=f"p2dR{sq}_{i}_{n}")
            def foldI(vE, n=n, eng2=None):
                eng2 = "pool" if n % 2 == 0 else "dve"
                vE.tensor_tensor(v["psii"][:, n, :], t["cctr"][:, n, :], t["bfA"][:, 4+n, :], ALU.mult)
                BAR(vE, eng2, vE.tensor_tensor(t["mt"][:], t["nccti"][:, n, :], t["bfA"][:, n, :], ALU.mult))
                return vE.tensor_tensor(v["psii"][:, n, :], v["psii"][:, n, :], t["mt"][:], ALU.subtract)
            A("pool" if eng == "dve" else "dve", foldI,
              reads=[f"ph0{n}", f"ph1{n}", f"cct{n}", "mt"],
              writes=[f"psii{n}", "mt", "xc0"], name=f"p2dI{sq}_{i}_{n}")
        for lt in range(2):
            pb = f"pb{6+lt}"
            def ymm2(te, lt=lt, pbn=pb, first=(i == 0)):
                ins = None
                seqm = []
                for fc in range(3):
                    seqm.append(("c_invc", fc, "xfr"))
                    seqm.append(("c_invs", fc, "yfi"))
                if not first:
                    for n in range(4):
                        seqm.append(("srb", n, None))
                        seqm.append(("sib", n, None))
                for ix, (mat, c, rh) in enumerate(seqm):
                    if mat in ("c_invc", "c_invs"):
                        lhs = t[mat][:, c, 128*lt:128*lt+128]
                        rhs = t[rh][:, c, :]
                    else:
                        lhs = t[mat][:, c, 1+128*lt:129+128*lt]
                        rhs = v["strb"][:, c, :] if mat == "srb" else v["nstib"][:, c, :]
                    ins = te.matmul(t[pbn][:], lhs, rhs, start=(ix == 0), stop=(ix == len(seqm)-1))
                return ins
            rdy = [f"yfr{c}" for c in range(3)] + [f"yfi{c}" for c in range(3)]
            if i > 0:
                rdy += ["strb", "nstib"] + [f"srb{n}" for n in range(4)] + [f"sib{n}" for n in range(4)]
            A("pe", ymm2, reads=rdy + ["c_invc", "c_invs"], writes=[pb], name=f"p2y{sq}_{i}_{lt}")
            A("act", lambda sc, lt=lt, pbn=pb: sc.copy(v["dtm"][lt][:], t[pbn][:]),
              reads=[pb], writes=[f"dtm{lt}", "mt" if lt == 0 else "rt_"],
              name=f"p2ye{sq}_{i}_{lt}")
        for n in range(4):
            def scanf(vE, n=n):
                lcr = t["lamc"][:, n:n+1]
                lci = t["lamc"][:, 4+n:5+n]
                nlci = t["lamc"][:, 8+n:9+n]
                BAR(vE, "dve", vE.scalar_tensor_tensor(v["psii"][:, n, :], v["sti"][:, n, :], lcr,
                                        v["psii"][:, n, :], ALU.mult, ALU.add))
                BAR(vE, "dve", vE.scalar_tensor_tensor(v["psii"][:, n, :], v["str_"][:, n, :], lci,
                                        v["psii"][:, n, :], ALU.mult, ALU.add))
                BAR(vE, "dve", vE.scalar_tensor_tensor(v["psir"][:, n, :], v["str_"][:, n, :], lcr,
                                        v["psir"][:, n, :], ALU.mult, ALU.add))
                BAR(vE, "dve", vE.scalar_tensor_tensor(v["str_"][:, n, :], v["sti"][:, n, :], nlci,
                                        v["psir"][:, n, :], ALU.mult, ALU.add))
                return vE.tensor_copy(v["sti"][:, n, :], v["psii"][:, n, :])
            A("dve", scanf, reads=[f"psir{n}", f"psii{n}", f"lamc{n}", "str_", "sti",
                                   "xc0", "xc1"],
              writes=["str_", "sti", f"psir{n}", f"psii{n}", "xc0", "xc1"],
              name=f"p2s{sq}_{i}_{n}")
        def mirf(vE):
            vE.tensor_copy(v["strb"][:], v["str_"][:])
            return vE.tensor_scalar_mul(v["nstib"][:], v["sti"][:], -1.0)
        A("dve", mirf, reads=["str_", "sti", "xc1", "bfB"], writes=["strb", "nstib", "bfB"],
          name=f"p2mir{sq}_{i}")
        for lt in range(2):
            for q in range(4):
                def trf(te, lt=lt, q=q):
                    return te.transpose(t["pb5"][:, 0:128], v["dtm"][lt][:, 128*q:128*q+128],
                                        t["c_identf"][:])
                A("pe", trf, reads=[f"dtm{lt}", "c_identf"], writes=["pb5"],
                  name=f"p2t{sq}_{i}_{lt}{q}")
                gt = 2048*sq + 256*i + 128*lt
                def evf(sc, q=q, gt=gt):
                    return sc.copy(v["dfm"][:, q, gt:gt+128], t["pb5"][:, 0:128])
                A("act", evf, reads=["pb5"], writes=[f"dfm{q}_{gt//512}", "H2"],
                  name=f"p2te{sq}_{i}_{lt}{q}")


def emit_p3_chunk(sch, t, v, dram, scr, tc):
    A = sch.add
    xc = t["xcA"] if tc % 2 == 0 else t["xcB"]
    xcb = f"xc{tc % 2}"
    sdma(sch, [xcb], xc[:], r128(scr["x1v"])[:, :, 512*tc:512*tc+512], name=f"p3_ldx{tc}")
    sch.ops[-1].deps = sorted(set(sch.ops[-1].deps) | {sch.last_writer[f"hx1v_{tc}"]})
    sdma(sch, ["bfB", "strb", "nstib"], t["bfB"][:],
         r128(scr["xln"])[:, :, 512*tc:512*tc+512], name=f"p3_ldl{tc}")
    sch.ops[-1].deps = sorted(set(sch.ops[-1].deps) | {sch.last_writer[f"hxln_{tc}"]})
    dfm_bufs = [f"dfm{q}_{tc}" for q in range(4)]
    for s in range(16):
        wxg = t["wxgA"] if s % 2 == 0 else t["wxgB"]
        wxgb = f"wxg{s % 2}"
        gdma(sch, wxgb, wxg[:], r128(dram["wxg"])[:, :, 256*s:256*s+256], name=f"p3_lwxg{tc}_{s}")
        wgt = t["wgtA"] if s % 2 == 0 else t["wgtB"]
        wgtb = f"wgt{s % 2}"
        gdma(sch, wgtb, wgt[:], r128(dram["wgate"])[:, :, 256*s:256*s+256], name=f"p3_lwgt{tc}_{s}")
        for j in range(2):
            pbx = f"pb{j}"
            def xgmm(te, j=j, wxg=wxg, pbn=pbx):
                ins = None
                for k in range(8):
                    ins = te.matmul(t[pbn][:], wxg[:, k, 128*j:128*j+128], t["bfB"][:, k, :],
                                    start=(k == 0), stop=(k == 7))
                return ins
            A("pe", xgmm, reads=[wxgb, "bfB"], writes=[pbx], name=f"p3xg{tc}_{s}_{j}")
            slot = 2*s + j
            A("act", lambda sc, slot=slot, pbn=pbx: sc.copy(v["xgP"][:, slot, :], t[pbn][:]),
              reads=[pbx], writes=[f"xgP{slot}", "H1"], name=f"p3xge{tc}_{s}_{j}")
            pbg = f"pb{2+j}"
            def gmm(te, j=j, wgt=wgt, pbn=pbg, tc=tc):
                ins = None
                for hc in range(4):
                    ins = te.matmul(t[pbn][:], wgt[:, hc, 128*j:128*j+128],
                                    v["dfm"][:, hc, 512*tc:512*tc+512],
                                    start=(hc == 0), stop=(hc == 3))
                return ins
            A("pe", gmm, reads=[wgtb] + dfm_bufs, writes=[pbg], name=f"p3g{tc}_{s}_{j}")
            A("dve", lambda vE, slot=slot, pbn=pbg:
                vE.tensor_tensor(v["xgP"][:, slot, :], t[pbn][:], v["xgP"][:, slot, :], ALU.mult),
              reads=[pbg, f"xgP{slot}"], writes=[f"xgP{slot}"], name=f"p3P{tc}_{s}_{j}")
    for s in range(16):
        wou = t["wouA"] if s % 2 == 0 else t["wouB"]
        woub = f"wou{s % 2}"
        gdma(sch, [woub] + (["dbrow"] if s % 2 == 0 else []), wou[:],
             r128(dram["wout"])[:, 2*s:2*s+2, :], name=f"p3_lwou{tc}_{s}")
        for oc in range(8):
            pb = f"pb{oc}"
            def betamm(te, s=s, oc=oc, wou=wou, pbn=pb):
                ins = None
                for hc in range(2):
                    ins = te.matmul(t[pbn][:], wou[:, hc, 128*oc:128*oc+128],
                                    v["xgP"][:, 2*s+hc, :],
                                    start=(s == 0 and hc == 0), stop=False)
                return ins
            A("pe", betamm, reads=[woub, f"xgP{2*s}", f"xgP{2*s+1}"] + ([] if s == 0 else [pb]),
              writes=[pb] if s == 0 else [], name=f"p3b{tc}_{s}_{oc}")
    for oc in range(8):
        pb = f"pb{oc}"
        def bstop(te, oc=oc, pbn=pb):
            return te.matmul(t[pbn][:], t["borow"][0:1, 128*oc:128*oc+128], t["onesrow"][:],
                             start=False, stop=True)
        A("pe", bstop, reads=["borow", "onesrow", pb], writes=[pb], name=f"p3bs{tc}_{oc}")
    for oc in range(8):
        pb = f"pb{oc}"
        A("dve", lambda vE, oc=oc, pbn=pb, xc=xc:
            vE.tensor_tensor(xc[:, oc, :], t[pbn][:], xc[:, oc, :], ALU.add),
          reads=[pb, xcb], writes=[xcb], name=f"p3x2_{tc}_{oc}")


def emit_p4_chunk(sch, t, v, dram, scr, yout, tc):
    A = sch.add
    xc = t["xcA"] if tc % 2 == 0 else t["xcB"]
    xcb = f"xc{tc % 2}"
    A("dve", lambda vE, xc=xc: vE.tensor_copy(t["bfA"][:], xc[:]),
      reads=[xcb, "bfA"], writes=["bfA"], name=f"p4_cast{tc}")
    A("act", lambda sc, xc=xc: sc.activation(t["bfB"][:], xc[:], AF.Square),
      reads=[xcb, "bfB"], writes=["bfB", "strb", "nstib"], name=f"p4_sq{tc}")
    def stats1(te):
        ins = None
        for k in range(8):
            ins = te.matmul(t["pb6"][:], t["onesbf"][:], t["bfA"][:, k, :],
                            start=(k == 0), stop=(k == 7))
        return ins
    A("pe", stats1, reads=["bfA", "onesbf"], writes=["pb6"], name=f"p4_s1_{tc}")
    def stats2(te):
        ins = None
        for k in range(8):
            ins = te.matmul(t["pb7"][:], t["onesbf"][:], t["bfB"][:, k, :],
                            start=(k == 0), stop=(k == 7))
        return ins
    A("pe", stats2, reads=["bfB", "onesbf"], writes=["pb7"], name=f"p4_s2_{tc}")
    A("act", lambda sc: sc.copy(t["mt"][:], t["pb6"][:]), reads=["pb6"],
      writes=["mt", "dtm0"], name=f"p4_m{tc}")
    def varfn(vE):
        BAR(vE, "dve", vE.tensor_tensor(t["rt_"][:], t["mt"][:], t["mt"][:], ALU.mult))
        return vE.tensor_tensor(t["rt_"][:], t["pb7"][:], t["rt_"][:], ALU.subtract)
    A("dve", varfn, reads=["mt", "pb7", "rt_"], writes=["rt_", "dtm1"], name=f"p4_var{tc}")
    def rfn(sc):
        BAR(sc, "act", sc.activation(t["st_"][:], t["rt_"][:], AF.Ln, bias=t["epsc"][:, 0:1]))
        return sc.activation(t["rt_"][:], t["st_"][:], AF.Exp, scale=-0.5)
    A("act", rfn, reads=["rt_", "epsc0", "st_"], writes=["rt_", "st_"], name=f"p4_r{tc}")
    def ufn(vE, xc=xc):
        ins = None
        for k in range(8):
            BAR(vE, "dve", vE.tensor_tensor(t["bfB"][:, k, :], xc[:, k, :], t["mt"][:], ALU.subtract))
            ins = vE.tensor_tensor(t["bfB"][:, k, :], t["bfB"][:, k, :], t["rt_"][:], ALU.mult)
        return ins
    A("dve", ufn, reads=[xcb, "mt", "rt_", "bfB"], writes=["bfB"], name=f"p4_u{tc}")
    for phase, mkey in enumerate(("m_hi", "m_lo")):
        for oc in range(8):
            mb = f"pql{oc % 2}"
            op = gdma(sch, mb, t["pql"][:, oc % 2, :, :],
                 r128(scr[mkey])[:, :, 128*oc:128*oc+128], name=f"p4_ldm{tc}_{phase}_{oc}")
            hp = "hi" if phase == 0 else "lo"
            sch.ops[op].deps = sorted(set(sch.ops[op].deps)
                | {sch.last_writer[f"hbm_m_{hp}_{r}_{h}"] for r in range(8) for h in range(2)})
            pb = f"pb{oc}"
            def ffn_mm(te, oc=oc, pbn=pb, phase=phase):
                ins = None
                for k in range(8):
                    ins = te.matmul(t[pbn][:], t["pql"][:, oc % 2, k, :], t["bfB"][:, k, :],
                                    start=(phase == 0 and k == 0), stop=False)
                if phase == 1:
                    ins = te.matmul(t[pbn][:], t["fbrow"][0:1, 128*oc:128*oc+128],
                                    t["onesrow"][:], start=False, stop=True)
                return ins
            A("pe", ffn_mm, reads=["bfB", "fbrow", "onesrow", mb] + ([pb] if phase else []),
              writes=[pb], name=f"p4_ffn{tc}_{phase}_{oc}")
    for oc in range(8):
        pb = f"pb{oc}"
        A("dve", lambda vE, oc=oc, pbn=pb, xc=xc:
            vE.scalar_tensor_tensor(xc[:, oc, :], t[pbn][:], 0.5, xc[:, oc, :], ALU.mult, ALU.add),
          reads=[pb, xcb], writes=[xcb], name=f"p4_x3_{tc}_{oc}")
    st_sdma(sch, [xcb], r128(yout)[:, :, 512*tc:512*tc+512], xc[:], name=f"p4_out{tc}")
    sch.last_writer[f"yout{tc}"] = sch.ops[-1].idx


def emit_debug(sch, t, v, scr, dbg, dbgset):
    if not dbgset:
        return
    for nm in ("x1v", "x1raw", "u1", "xln"):
        if nm in dbgset:
            st_sdma(sch, [], dbg[nm], scr[nm], name=f"dbg_{nm}")
    for nm, key in (("pthi", "pt_hi"), ("ptlo", "pt_lo"), ("qthi", "qt_hi"),
                    ("wrhi", "wrhi"), ("wrlo", "wrlo"), ("mlo", "m_lo")):
        if nm in dbgset:
            st_sdma(sch, [], dbg[nm], scr[key], name=f"dbg_{nm}")
    if "mmat" in dbgset:
        op = st_sdma(sch, [], dbg["mmat"], scr["m_hi"], name="dbg_mmat")
        sch.ops[op].deps = sorted(set(sch.ops[op].deps)
                                  | {sch.last_writer[f"hbm_m_hi_{r}_{h}"] for r in range(8) for h in range(2)})
    if "xn" in dbgset:
        st_sdma(sch, [f"xn{s}" for s in range(32)],
                dbg["xn"].rearrange("(a p) h -> p a h", p=128), v["xn"], name="dbg_xn")
    if "kp" in dbgset:
        st_sdma(sch, ["kp0", "kp1"], dbg["kp"].rearrange("(a p) h -> p a h", p=128),
                t["kp"][:], name="dbg_kp")
    if "kfcr" in dbgset:
        st_sdma(sch, [f"kfcr{c}" for c in range(3)],
                dbg["kfcr"].rearrange("(a p) h -> p a h", p=128), t["kfcr"][:], name="dbg_kfcr")
    if "kfci" in dbgset:
        st_sdma(sch, [f"kfci{c}" for c in range(3)],
                dbg["kfci"].rearrange("(a p) h -> p a h", p=128), t["kfci"][:], name="dbg_kfci")
    if "srb" in dbgset:
        st_sdma(sch, [f"srb{n}" for n in range(4)],
                dbg["srb"].rearrange("(a p) h -> p a h", p=128), t["srb"][:], name="dbg_srb")
    if "sib" in dbgset:
        st_sdma(sch, [f"sib{n}" for n in range(4)],
                dbg["sib"].rearrange("(a p) h -> p a h", p=128), t["sib"][:], name="dbg_sib")
    if "cctr" in dbgset:
        st_sdma(sch, [f"cct{n}" for n in range(4)],
                dbg["cctr"].rearrange("(a p) h -> p a h", p=128), t["cctr"][:], name="dbg_cctr")
    if "nccti" in dbgset:
        st_sdma(sch, [f"cct{n}" for n in range(4)],
                dbg["nccti"].rearrange("(a p) h -> p a h", p=128), t["nccti"][:], name="dbg_nccti")
    if "dfm" in dbgset:
        st_sdma(sch, [f"dfm{q}_{c}" for q in range(4) for c in range(8)],
                dbg["dfm"].rearrange("(a p) tt -> p a tt", p=128), v["dfm"], name="dbg_dfm")


def make_inputs(inputs, core):
    cst = host_constants()
    idx = inputs["L0_attn_idx"]
    x0 = np.ascontiguousarray(
        idx[BPC*core:BPC*core+BPC].reshape(BPC*SEQ, D_MODEL).T)
    m = {
        "x0": x0,
        "wxg": inputs["L0_attn_Wxg"],
        "wdin": inputs["L0_attn_Wdin"],
        "wgate": inputs["L0_attn_Wgate"],
        "wout": inputs["L0_attn_Wout"],
        "wfr": inputs["L0_ffn_w_freq_real"],
        "wfi": inputs["L0_ffn_w_freq_imag"],
        "crt": np.ascontiguousarray(inputs["L0_attn_CR"].T),
        "cit": np.ascontiguousarray(inputs["L0_attn_CI"].T),
        "lr": np.ascontiguousarray(inputs["L0_attn_LR"].reshape(4, 128).T),
        "li": np.ascontiguousarray(inputs["L0_attn_LI"].reshape(4, 128).T),
        "dvec": inputs["L0_attn_D"].reshape(1, 512),
        "n1w": np.ascontiguousarray(inputs["L0_attn_n1_w"].reshape(8, 128).T),
        "n1b": np.ascontiguousarray(inputs["L0_attn_n1_b"].reshape(8, 128).T),
        "n2w": inputs["L0_attn_n2_w"].reshape(1, 512),
        "n2b": inputs["L0_attn_n2_b"].reshape(1, 512),
        "fbias": inputs["L0_ffn_bias"].reshape(1, D_MODEL),
        "bout": inputs["L0_attn_bout"].reshape(1, D_MODEL),
        "cmat": cst["cmat"], "smat": cst["smat"],
        "fwdc": cst["fwdc"], "fwds": cst["fwds"],
        "invc": cst["invc"], "invs": cst["invs"],
        "identf": cst["identf"],
        "jrow": cst["jrow"],
    }
    return {k: np.ascontiguousarray(val) for k, val in m.items()}


def kernel(**inputs):
    nc = build_program()
    in_maps = [make_inputs(inputs, c) for c in range(NCORES)]
    res = run_bass_kernel_spmd(nc, in_maps, core_ids=list(range(NCORES)))
    outs = []
    for c in range(NCORES):
        y = np.asarray(res.results[c]["y"])
        outs.append(np.ascontiguousarray(y.T).reshape(BPC, SEQ, D_MODEL))
    return np.concatenate(outs, 0)
